# version 01
# baseline (speedup 1.0000x reference)
import numpy as np
import ml_dtypes
import contextlib
import concourse.bass as bass
import concourse.mybir as mybir
from concourse.alu_op_type import AluOpType
from concourse.bass_utils import run_bass_kernel_spmd

F32 = mybir.dt.float32
BF16 = mybir.dt.bfloat16
AF = mybir.ActivationFunctionType
AX = mybir.AxisListType

ENGINES = ("pe", "dve", "act", "pool", "sync")
ENG_ATTR = {"pe": "tensor", "dve": "vector", "act": "scalar",
            "pool": "gpsimd", "sync": "sync"}

T = 2048
D = 1024
G = 8
NCH = 4
CH = 512


class _Buf:
    __slots__ = ("name", "ready", "readers")

    def __init__(self, name):
        self.name = name
        self.ready = None
        self.readers = []


class Emitter:
    def __init__(self, nc):
        self.nc = nc
        self.ops = {e: [] for e in ENGINES}
        self.cnt = {e: 0 for e in ENGINES}
        self.dcnt = {e: 0 for e in ENGINES}
        self.bufs = {}
        self.alias = {}

    def buf(self, name):
        name = self.alias.get(name, name)
        b = self.bufs.get(name)
        if b is None:
            b = self.bufs[name] = _Buf(name)
        return b

    def _deps(self, eng, reads, writes):
        deps = {}
        for b in reads:
            if b.ready is not None:
                s, v = b.ready
                if v > deps.get(s, -1):
                    deps[s] = v
        for b in writes:
            toks = list(b.readers)
            if b.ready is not None:
                toks.append(b.ready)
            for s, v in toks:
                if v > deps.get(s, -1):
                    deps[s] = v
        return sorted(deps.items())

    def op(self, eng, fn, reads=(), writes=()):
        reads = [self.buf(b) if isinstance(b, str) else b for b in reads]
        writes = [self.buf(b) if isinstance(b, str) else b for b in writes]
        deps = self._deps(eng, reads, writes)
        self.cnt[eng] += 1
        tok = ("s_" + eng, self.cnt[eng])
        for b in writes:
            b.ready = tok
            b.readers = []
        for b in reads:
            b.readers.append(tok)
        self.ops[eng].append((deps, fn, "s_" + eng, 1))
        return tok

    def dma(self, eng, fn, reads=(), writes=()):
        reads = [self.buf(b) if isinstance(b, str) else b for b in reads]
        writes = [self.buf(b) if isinstance(b, str) else b for b in writes]
        deps = self._deps(eng, reads, writes)
        self.dcnt[eng] += 16
        tok = ("d_" + eng, self.dcnt[eng])
        for b in writes:
            b.ready = tok
            b.readers = []
        for b in reads:
            b.readers.append(tok)
        self.ops[eng].append((deps, fn, "d_" + eng, 16))
        return tok

    def wait(self, eng, toks):
        deps = {}
        for s, v in toks:
            if v > deps.get(s, -1):
                deps[s] = v
        self.ops[eng].append((sorted(deps.items()), None, None, 0))

    def barrier_tokens(self):
        toks = [("s_" + e, self.cnt[e]) for e in ENGINES if self.cnt[e] > 0]
        toks += [("d_" + e, self.dcnt[e]) for e in ENGINES if self.dcnt[e] > 0]
        return toks

    def build(self, stack):
        nc = self.nc
        sems = {}
        for e in ENGINES:
            if self.cnt[e] > 0:
                sems["s_" + e] = stack.enter_context(nc.semaphore("s_" + e))
            if self.dcnt[e] > 0:
                sems["d_" + e] = stack.enter_context(nc.semaphore("d_" + e))
        block = stack.enter_context(nc.Block())
        last_wait = {e: {} for e in ENGINES}

        def make_replay(eng_name):
            def replay(engine):
                lw = last_wait[eng_name]
                for deps, fn, incsem, inc in self.ops[eng_name]:
                    for s, v in deps:
                        if lw.get(s, 0) < v:
                            engine.wait_ge(sems[s], v)
                            lw[s] = v
                    if fn is not None:
                        instr = fn(engine)
                        if incsem is not None:
                            instr.then_inc(sems[incsem], inc)
            return replay

        for e in ENGINES:
            if self.ops[e]:
                getattr(block, ENG_ATTR[e])(make_replay(e))
        return sems


class Ctx:

    def __init__(self, name):
        self.nc = bass.Bass("TRN2", target_bir_lowering=False)
        self.em = Emitter(self.nc)
        self.stack = contextlib.ExitStack()
        self.S = {}
        self.ins = {}
        self.outs = {}
        self.dump_req = []

    def inp(self, name, arr, dt=F32):
        arr = np.ascontiguousarray(arr)
        ap = self.nc.dram_tensor(name, list(arr.shape), dt, kind="ExternalInput").ap()
        self.ins[name] = (ap, arr)
        return ap

    def out(self, name, shape, dt=F32):
        ap = self.nc.dram_tensor(name, list(shape), dt, kind="ExternalOutput").ap()
        self.outs[name] = ap
        return ap

    def sb(self, name, shape, dt=F32):
        t = self.stack.enter_context(self.nc.sbuf_tensor(name, list(shape), dt))
        self.S[name] = t
        return t

    def ps(self, name, shape, dt=F32):
        t = self.stack.enter_context(self.nc.psum_tensor(name, list(shape), dt))
        self.S[name] = t
        return t

    def reg(self, name, ap, bufname):
        self.S[name] = ap
        self.em.alias[name] = bufname

    def pools(self, nF, nB):
        pf = self.sb("poolF", (128, nF * T), F32)
        pb = self.sb("poolB", (128, nB * T), BF16)
        for j in range(nF):
            self.S[f"pf{j}"] = pf[:, j * T:(j + 1) * T]
        for j in range(nB):
            self.S[f"pb{j}"] = pb[:, j * T:(j + 1) * T]

    def load(self, dst_ap, src_ap, wr, rd=(), q="sync"):
        return self.em.dma(q, lambda e: e.dma_start(out=dst_ap, in_=src_ap),
                           reads=rd, writes=wr)

    def store(self, dst_ap, src_ap, rd, q="pool"):
        return self.em.dma(q, lambda e: e.dma_start(out=dst_ap, in_=src_ap),
                           reads=rd)

    def mm(self, psum_ap, lhsT_ap, rhs_ap, start, stop, reads, writes):
        return self.em.op("pe", lambda e: e.matmul(psum_ap, lhsT_ap, rhs_ap,
                          start=start, stop=stop), reads=reads, writes=writes)

    def ew(self, eng, fn, reads, writes):
        return self.em.op(eng, fn, reads=reads, writes=writes)


def make_constants():
    c = {}
    c["identb"] = np.eye(128, dtype=np.float32)
    c["identf"] = np.eye(128, dtype=np.float32)
    c["ones_row"] = np.ones((1, 128), dtype=np.float32)
    c["ones_colb"] = np.ones((128, 1), dtype=np.float32)
    Dh = 64
    inv_freq = 1.0 / (10000.0 ** (np.arange(0, Dh, 2, dtype=np.float32) / Dh))
    t = np.arange(T, dtype=np.float32)
    freqs = np.einsum("i,j->ij", t, inv_freq)
    emb = np.concatenate([freqs, freqs], axis=-1)
    c["ropeCos"] = np.ascontiguousarray(np.cos(emb).T).astype(ml_dtypes.bfloat16)
    c["ropeSin"] = np.ascontiguousarray(np.sin(emb).T).astype(ml_dtypes.bfloat16)
    masks = []
    for off in (0, 128, 256, 384):
        qi = np.arange(128)[:, None] + off
        ki = np.arange(512)[None, :]
        masks.append(np.where(ki <= qi, 0.0, -1e30).astype(ml_dtypes.bfloat16))
    c["cmask"] = np.stack(masks)
    c["eps_col"] = np.full((128, 1), 1e-5, np.float32)
    oh = np.zeros((16, 16 * 128), np.float32)
    for n in range(16):
        oh[n, n * 128:(n + 1) * 128] = 1.0
    c["onehot16"] = oh
    return c


def emit_load_x_dmajor(ctx, x_ap):
    em, S = ctx.em, ctx.S
    ctx.ps("ptr0", (128, 128), F32)
    ctx.ps("ptr1", (128, 128), F32)
    for tt in range(T // 128):
        slot = "pf0"
        stg = S["pf0"][:, (tt % 2) * D:(tt % 2) * D + D]
        ctx.load(stg, x_ap[tt * 128:(tt + 1) * 128, :], wr=[slot])
        for g in range(G):
            pname = f"ptr{(tt * G + g) % 2}"
            em.op("pe", lambda e, stg=stg, g=g, pname=pname: e.transpose(
                S[pname][:], stg[:, g * 128:(g + 1) * 128], S["identf"][:]),
                reads=[slot, "identf"], writes=[pname])
            dst = S["X"][:, g * T + tt * 128: g * T + (tt + 1) * 128]
            if g % 2 == 0:
                em.op("dve", lambda e, dst=dst, pname=pname: e.tensor_copy(dst, S[pname][:]),
                      reads=[pname], writes=["X"])
            else:
                em.op("act", lambda e, dst=dst, pname=pname: e.copy(dst, S[pname][:]),
                      reads=[pname], writes=["X"])


def emit_store_x_dmajor(ctx, y_ap, src="X"):
    em, S = ctx.em, ctx.S
    if "p_w0" in S:
        ctx.reg("ptr0", S["p_w0"][:, 0:128], "p_w0")
        ctx.reg("ptr1", S["p_w1"][:, 0:128], "p_w1")
    else:
        ctx.ps("ptr0", (128, 128), F32)
        ctx.ps("ptr1", (128, 128), F32)
    toks = []
    for tt in range(T // 128):
        slot = "pf0"
        stg = S["pf0"][:, (tt % 2) * D:(tt % 2) * D + D]
        for g in range(G):
            pname = f"ptr{(tt * G + g) % 2}"
            srcap = S[src][:, g * T + tt * 128: g * T + (tt + 1) * 128]
            pap = S[pname] if S[pname].__class__.__name__ == "AP" else S[pname][:]
            em.op("pe", lambda e, srcap=srcap, pap=pap: e.transpose(
                pap, srcap, S["identf"][:]),
                reads=[src, "identf"], writes=[pname])
            if g % 2 == 0:
                em.op("dve", lambda e, pap=pap, stg=stg, g=g: e.tensor_copy(
                    stg[:, g * 128:(g + 1) * 128], pap),
                    reads=[pname], writes=[slot])
            else:
                em.op("act", lambda e, pap=pap, stg=stg, g=g: e.copy(
                    stg[:, g * 128:(g + 1) * 128], pap),
                    reads=[pname], writes=[slot])
        toks.append(ctx.store(y_ap[tt * 128:(tt + 1) * 128, :], stg, rd=[slot]))
    return toks


def reg_ln(ctx):
    S = ctx.S
    ctx.reg("st_sq", S["pf1"][0:1, :], "pf1")
    ctx.reg("lntmp", S["pf1"], "pf1")
    ctx.reg("st_m", S["pf2"][0:1, :], "pf2")
    ctx.reg("m_bc", S["pf2"], "pf2")
    ctx.reg("st_rs", S["pf0"][0:1, :], "pf0")
    ctx.reg("rs_bc", S["pf0"], "pf0")
    ctx.reg("lnxb", S["pb0"], "pb0")
    ctx.reg("lnsq", S["pb1"], "pb1")
    if "plnr" not in S:
        ctx.ps("plnr", (1, 512), F32)
        ctx.ps("plnb", (128, 512), F32)


def emit_ln_stats(ctx, src="X"):
    em, S = ctx.em, ctx.S
    sumx, sumq, rrow = S["st_m"], S["st_sq"], S["st_rs"]
    for g in range(G):
        xg = S[src][:, g * T:(g + 1) * T]
        em.op("dve", lambda e, xg=xg: e.tensor_copy(S["lnxb"][:], xg),
              reads=[src], writes=["lnxb"])
        em.op("act", lambda e, xg=xg: e.activation(S["lnsq"][:], xg, AF.Square),
              reads=[src], writes=["lnsq"])
        for cc in range(NCH):
            for srcb, rowap, rbuf in (("lnxb", sumx, "st_m"), ("lnsq", sumq, "st_sq")):
                em.op("pe", lambda e, srcb=srcb, cc=cc: e.matmul(
                    S["plnr"][:], S["onesb_sb"][:], S[srcb][:, cc * CH:(cc + 1) * CH],
                    start=True, stop=True),
                    reads=[srcb, "onesb_sb"], writes=["plnr"])
                dst = rowap[:, cc * CH:(cc + 1) * CH]
                if g == 0:
                    em.op("dve", lambda e, dst=dst: e.tensor_copy(dst, S["plnr"][:]),
                          reads=["plnr"], writes=[rbuf])
                else:
                    em.op("dve", lambda e, dst=dst: e.tensor_tensor(
                        dst, dst, S["plnr"][:], AluOpType.add),
                        reads=["plnr", rbuf], writes=[rbuf])
    em.op("dve", lambda e: e.tensor_scalar_mul(sumx, sumx, 1.0 / D),
          reads=["st_m"], writes=["st_m"])
    em.op("dve", lambda e: e.tensor_tensor(rrow, sumx, sumx, AluOpType.mult),
          reads=["st_m"], writes=["st_rs"])
    em.op("dve", lambda e: e.scalar_tensor_tensor(rrow, sumq, 1.0 / D, rrow,
          AluOpType.mult, AluOpType.subtract), reads=["st_sq", "st_rs"], writes=["st_rs"])
    em.op("act", lambda e: e.activation(rrow, rrow, AF.Sqrt, bias=S["epscol"][0:1, :]),
          reads=["st_rs", "epscol"], writes=["st_rs"])
    em.op("dve", lambda e: e.reciprocal(rrow, rrow), reads=["st_rs"], writes=["st_rs"])
    for cc in range(NCH):
        for rowap, dname, dbuf in ((sumx, "m_bc", "pf2"), (rrow, "rs_bc", "pf0")):
            em.op("pe", lambda e, rowap=rowap, cc=cc: e.matmul(
                S["plnb"][:], S["ones_row_f"][:], rowap[:, cc * CH:(cc + 1) * CH],
                start=True, stop=True),
                reads=["st_m", "st_rs", "ones_row_f"], writes=["plnb"])
            em.op("dve", lambda e, dname=dname, cc=cc: e.tensor_copy(
                S[dname][:, cc * CH:(cc + 1) * CH], S["plnb"][:]),
                reads=["plnb"], writes=[dbuf])


def emit_ln_apply_g(ctx, g, src, dst_ap, dst_buf, w_col=None, b_col=None, w_buf=()):
    em, S = ctx.em, ctx.S
    xg = S[src][:, g * T:(g + 1) * T]
    em.op("dve", lambda e, xg=xg: e.tensor_tensor(S["lntmp"][:], xg,
          S["m_bc"][:], AluOpType.subtract),
          reads=[src, "m_bc"], writes=["lntmp"])
    if w_col is None:
        em.op("dve", lambda e: e.tensor_tensor(dst_ap, S["lntmp"][:],
              S["rs_bc"][:], AluOpType.mult),
              reads=["lntmp", "rs_bc"], writes=[dst_buf])
    else:
        em.op("dve", lambda e: e.tensor_tensor(S["lntmp"][:], S["lntmp"][:],
              S["rs_bc"][:], AluOpType.mult),
              reads=["lntmp", "rs_bc"], writes=["lntmp"])
        em.op("dve", lambda e: e.tensor_scalar(dst_ap, S["lntmp"][:], w_col, b_col,
              AluOpType.mult, AluOpType.add),
              reads=["lntmp"] + list(w_buf), writes=[dst_buf])



def emit_stage1(ctx, W):
    em, S = ctx.em, ctx.S
    xpT = ctx.inp("w_xpT", W["xpT"])
    dtwT = ctx.inp("w_dtwT", W["dtwT"])
    BpT = ctx.inp("w_BpT", W["BpT"])
    CpT = ctx.inp("w_CpT", W["CpT"])
    Alog = ctx.inp("w_Alog", W["Alog"])
    Dv = ctx.inp("w_D", W["D"])
    dtb = ctx.inp("w_dtb", W["dtb"])

    WCOL = 114
    ctx.sb("mam_w", (128, G * WCOL), F32)
    for g in range(G):
        base = g * WCOL
        ctx.load(S["mam_w"][:, base:base + 64], xpT[g * 128:(g + 1) * 128, :], wr=["mam_w"])
        ctx.load(S["mam_w"][:, base + 64:base + 80], BpT[g * 128:(g + 1) * 128, :], wr=["mam_w"])
        ctx.load(S["mam_w"][:, base + 80:base + 96], CpT[g * 128:(g + 1) * 128, :], wr=["mam_w"])
        ctx.load(S["mam_w"][:, base + 96:base + 112], Alog[g * 128:(g + 1) * 128, :], wr=["mam_w"])
        ctx.load(S["mam_w"][:, base + 112:base + 113], Dv[g * 128:(g + 1) * 128, :], wr=["mam_w"])
        ctx.load(S["mam_w"][:, base + 113:base + 114], dtb[g * 128:(g + 1) * 128, :], wr=["mam_w"])
    ctx.sb("mam_wb", (128, G * WCOL), BF16)
    em.op("dve", lambda e: e.tensor_copy(S["mam_wb"][:], S["mam_w"][:]),
          reads=["mam_w"], writes=["mam_wb"])
    ctx.sb("mam_A", (128, G * 16), F32)
    for g in range(G):
        em.op("act", lambda e, g=g: e.activation(
            S["mam_A"][:, g * 16:(g + 1) * 16],
            S["mam_w"][:, g * WCOL + 96:g * WCOL + 112], AF.Exp),
            reads=["mam_w"], writes=["mam_A"])
    em.op("dve", lambda e: e.tensor_scalar_mul(S["mam_A"][:], S["mam_A"][:], -1.0),
          reads=["mam_A"], writes=["mam_A"])
    ctx.reg("dtwT_f", S["pf2"][0:64, 0:D], "pf2")
    ctx.reg("dtwT_b", S["pb7"][0:64, 0:D], "pb7")
    ctx.load(S["dtwT_f"], dtwT[:, :], wr=["dtwT_f"])
    em.op("dve", lambda e: e.tensor_copy(S["dtwT_b"], S["dtwT_f"]),
          reads=["dtwT_f"], writes=["dtwT_b"])

    emit_ln_stats(ctx, "X")
    ctx.sb("lnb", (128, G * T), BF16)
    for g in range(G):
        emit_ln_apply_g(ctx, g, "X", S["lnb"][:, g * T:(g + 1) * T], "lnb")

    ctx.reg("rT", S["pb6"][0:64, :], "pb6")
    ctx.reg("BmT", S["pb6"][0:16, :], "pb6")
    ctx.reg("CmT", S["pb7"][0:16, :], "pb7")
    oh_ap = ctx.inp("c_onehot16", ctx.consts["onehot16"])
    ctx.sb("oh16b", (16, 16 * 128), BF16)
    ctx.reg("oh16f", S["pf1"][0:16, :], "pf1")
    ctx.load(S["oh16f"], oh_ap[:, :], wr=["oh16f"])
    em.op("dve", lambda e: e.tensor_copy(S["oh16b"][:], S["oh16f"]),
          reads=["oh16f"], writes=["oh16b"])
    ctx.ps("p_r", (64, CH), F32)
    for cc in range(NCH):
        for g in range(G):
            em.op("pe", lambda e, g=g, cc=cc: e.matmul(
                S["p_r"][:], S["mam_wb"][:, g * WCOL:g * WCOL + 64],
                S["lnb"][:, g * T + cc * CH:g * T + (cc + 1) * CH],
                start=(g == 0), stop=(g == G - 1)),
                reads=["mam_wb", "lnb"], writes=["p_r"])
        em.op("act", lambda e, cc=cc: e.copy(S["rT"][:, cc * CH:(cc + 1) * CH],
              S["p_r"][:]), reads=["p_r"], writes=["rT"])
    ctx.ps("p_dl", (128, CH), F32)
    ctx.sb("delta", (128, G * T), BF16)
    ctx.reg("dxf", S["pf1"][:, 0:CH], "pf1")
    ctx.reg("dxa", S["pf1"][:, CH:2 * CH], "pf1")
    for g in range(G):
        for cc in range(NCH):
            em.op("pe", lambda e, g=g, cc=cc: e.matmul(
                S["p_dl"][:], S["dtwT_b"][:, g * 128:(g + 1) * 128],
                S["rT"][:, cc * CH:(cc + 1) * CH], start=True, stop=True),
                reads=["dtwT_b", "rT"], writes=["p_dl"])
            dtbcol = S["mam_w"][:, g * WCOL + 113:g * WCOL + 114]
            em.op("dve", lambda e, dtbcol=dtbcol: e.tensor_scalar(
                S["dxf"], S["p_dl"][:], dtbcol, None, AluOpType.add),
                reads=["p_dl", "mam_w"], writes=["dxf"])
            em.op("act", lambda e: e.activation(S["dxa"], S["dxf"], AF.Abs),
                  reads=["dxf"], writes=["dxa"])
            em.op("act", lambda e: e.activation(S["dxa"], S["dxa"], AF.Exp,
                  scale=-1.0), reads=["dxa"], writes=["dxa"])
            em.op("act", lambda e: e.activation(S["dxa"], S["dxa"], AF.Ln,
                  bias=1.0), reads=["dxa"], writes=["dxa"])
            em.op("dve", lambda e: e.tensor_scalar_max(S["dxf"], S["dxf"], 0.0),
                  reads=["dxf"], writes=["dxf"])
            em.op("dve", lambda e: e.tensor_tensor(S["dxf"], S["dxf"],
                  S["dxa"], AluOpType.add), reads=["dxf", "dxa"], writes=["dxf"])
            dl = S["delta"][:, g * T + cc * CH:g * T + (cc + 1) * CH]
            em.op("dve", lambda e, dl=dl: e.tensor_scalar(dl, S["dxf"], 0.001, 0.1,
                  AluOpType.max, AluOpType.min), reads=["dxf"], writes=["delta"])

    ctx.ps("p_bc", (32, CH), F32)
    for cc in range(NCH):
        for nm, off in (("BmT", 64), ("CmT", 80)):
            for g in range(G):
                em.op("pe", lambda e, g=g, cc=cc, off=off: e.matmul(
                    S["p_bc"][0:16, :], S["mam_wb"][:, g * WCOL + off:g * WCOL + off + 16],
                    S["lnb"][:, g * T + cc * CH:g * T + (cc + 1) * CH],
                    start=(g == 0), stop=(g == G - 1)),
                    reads=["mam_wb", "lnb"], writes=["p_bc"])
            em.op("dve", lambda e, nm=nm, cc=cc: e.tensor_copy(
                S[nm][:, cc * CH:(cc + 1) * CH], S["p_bc"][0:16, :]),
                reads=["p_bc"], writes=[nm])

    for g in range(G):
        xg = S["X"][:, g * T:(g + 1) * T]
        dcol = S["mam_w"][:, g * WCOL + 112:g * WCOL + 113]
        em.op("dve", lambda e, g=g, dcol=dcol: e.tensor_scalar(
            S["lntmp"][:], S["lnb"][:, g * T:(g + 1) * T], dcol, None,
            AluOpType.mult), reads=["lnb", "mam_w"], writes=["lntmp"])
        em.op("dve", lambda e, xg=xg: e.tensor_tensor(xg, xg, S["lntmp"][:],
              AluOpType.add), reads=["lntmp", "X"], writes=["X"])

    for g in range(G):
        em.op("dve", lambda e, g=g: e.tensor_tensor(
            S["lnb"][:, g * T:(g + 1) * T], S["delta"][:, g * T:(g + 1) * T],
            S["lnb"][:, g * T:(g + 1) * T], AluOpType.mult),
            reads=["delta", "lnb"], writes=["lnb"])

    ctx.reg("Bbc", S["pb0"], "pb0")
    ctx.reg("Cbc", S["pb1"], "pb1")
    ctx.reg("Abar", S["pb2"], "pb2")
    ctx.reg("hbuf", S["pb3"], "pb3")
    ctx.reg("hscan", S["pb4"], "pb4")
    ctx.reg("ycon", S["pb5"], "pb5")
    ctx.ps("p_bb", (128, CH), F32)
    for n in range(16):
        for nm, dst, dbuf in (("BmT", "Bbc", "pb0"), ("CmT", "Cbc", "pb1")):
            for hh in range(NCH):
                em.op("pe", lambda e, nm=nm, n=n, hh=hh: e.matmul(
                    S["p_bb"][:], S["oh16b"][:, n * 128:(n + 1) * 128],
                    S[nm][:, hh * CH:(hh + 1) * CH],
                    start=True, stop=True),
                    reads=[nm, "oh16b"], writes=["p_bb"])
                em.op("dve", lambda e, dst=dst, hh=hh: e.tensor_copy(
                    S[dst][:, hh * CH:(hh + 1) * CH], S["p_bb"][:]),
                    reads=["p_bb"], writes=[dbuf])
        for g in range(G):
            acol = S["mam_A"][:, g * 16 + n:g * 16 + n + 1]
            em.op("act", lambda e, g=g, acol=acol: e.activation(
                S["Abar"][:], S["delta"][:, g * T:(g + 1) * T], AF.Exp, scale=acol),
                reads=["delta", "mam_A"], writes=["Abar"])
            em.op("dve", lambda e, g=g: e.tensor_tensor(
                S["hbuf"][:], S["lnb"][:, g * T:(g + 1) * T], S["Bbc"][:],
                AluOpType.mult), reads=["lnb", "Bbc"], writes=["hbuf"])
            em.op("dve", lambda e: e.tensor_tensor_scan(
                S["hscan"][:], S["Abar"][:], S["hbuf"][:], 0.0,
                AluOpType.mult, AluOpType.add),
                reads=["Abar", "hbuf"], writes=["hscan"])
            em.op("dve", lambda e: e.tensor_tensor(S["ycon"][:], S["hscan"][:],
                  S["Cbc"][:], AluOpType.mult), reads=["hscan", "Cbc"], writes=["ycon"])
            xg = S["X"][:, g * T:(g + 1) * T]
            em.op("dve", lambda e, xg=xg: e.tensor_tensor(xg, xg, S["ycon"][:],
                  AluOpType.add), reads=["ycon", "X"], writes=["X"])



def prep_weights(inputs):
    W = {}
    W["xpT"] = inputs["L0_attn_x_proj_w"].T
    W["dtwT"] = inputs["L0_attn_dt_proj_w"].T
    W["BpT"] = inputs["L0_attn_B_proj_w"].T
    W["CpT"] = inputs["L0_attn_C_proj_w"].T
    W["Alog"] = inputs["L0_attn_A_log"]
    W["D"] = inputs["L0_attn_D"].reshape(D, 1)
    W["dtb"] = inputs["L0_attn_dt_proj_b"].reshape(D, 1)
    return W


def ctx_common(ctx, consts):
    cid = ctx.inp("c_identf", consts["identf"])
    onesc = ctx.inp("c_onescol", consts["ones_colb"])
    onesr = ctx.inp("c_onesrow", consts["ones_row"])
    ctx.sb("identf", (128, 128), F32)
    ctx.sb("epscol", (128, 1), F32)
    eps_ap = ctx.inp("c_epscol", consts["eps_col"])
    ctx.load(ctx.S["epscol"][:], eps_ap[:, :], wr=["epscol"])
    ctx.sb("onesb_sb", (128, 1), BF16)
    ctx.sb("ones_colf", (128, 1), F32)
    ctx.sb("ones_row_f", (1, 128), F32)
    ctx.sb("ones_row_b", (1, 128), BF16)
    ctx.load(ctx.S["identf"][:], cid[:, :], wr=["identf"])
    ctx.load(ctx.S["ones_colf"][:], onesc[:, :], wr=["ones_colf"])
    ctx.load(ctx.S["ones_row_f"][:], onesr[:, :], wr=["ones_row_f"])
    ctx.em.op("dve", lambda e: e.tensor_copy(ctx.S["onesb_sb"][:], ctx.S["ones_colf"][:]),
              reads=["ones_colf"], writes=["onesb_sb"])
    ctx.em.op("dve", lambda e: e.tensor_copy(ctx.S["ones_row_b"][:], ctx.S["ones_row_f"][:]),
              reads=["ones_row_f"], writes=["ones_row_b"])
    reg_ln(ctx)


def build_stage1_program(x_shard_shape, W, consts, dump=()):
    ctx = Ctx("s1")
    ctx.consts = consts
    x_ap = ctx.inp("x", np.zeros(x_shard_shape, np.float32))
    o_ap = ctx.out("xdm", (128, G * T))
    ctx.sb("X", (128, G * T), F32)
    ctx.pools(3, 8)
    ctx_common(ctx, consts)
    emit_load_x_dmajor(ctx, x_ap)
    emit_stage1(ctx, W)
    tok = ctx.store(o_ap[:, :], ctx.S["X"][:], rd=["X"])
    emit_dumps(ctx, dump)
    ctx.em.wait("pool", [tok])
    ctx.em.build(ctx.stack)
    return ctx


def emit_dumps(ctx, dump):
    for nm in dump:
        t = ctx.S[nm]
        shape = [t.shape[0], t.shape[1]]
        try:
            dt = t.dtype
        except Exception:
            dt = F32
        ap = ctx.out("dump_" + nm, shape, dt)
        ctx.em.wait("pool", ctx.em.barrier_tokens())
        ctx.store(ap[:, :] if hasattr(t, "ap") else ap[:, :],
                  t[:] if hasattr(t, "ap") else t, rd=[nm])





def prep_weights2(inputs):
    W = {}
    W1 = inputs["L0_ffn_W1"]
    W["W1L"] = np.ascontiguousarray(
        W1.reshape(8, 8, 128, 32, 128).transpose(0, 3, 2, 1, 4)
        .reshape(8, 32, 128, 1024))
    W["W2L"] = np.ascontiguousarray(
        inputs["L0_ffn_W2"].reshape(8, 32, 128, 8, 128).transpose(0, 3, 2, 1, 4)
        .reshape(8, 8, 128, 4096))
    W["b1c"] = np.ascontiguousarray(
        inputs["L0_ffn_b1"].reshape(8, 32, 128).transpose(0, 2, 1))
    W["b2T"] = np.ascontiguousarray(inputs["L0_ffn_b2"].T)
    W["Wr"] = inputs["L0_ffn_Wr"]
    W["br"] = inputs["L0_ffn_br"].reshape(8, 1)
    oh = np.zeros((8, 8 * 128), np.float32)
    for e in range(8):
        oh[e, e * 128:(e + 1) * 128] = 1.0
    W["onehot8"] = oh
    return W


def emit_stage2(ctx, W):
    em, S = ctx.em, ctx.S
    W1L = ctx.inp("w_W1L", W["W1L"])
    W2d = ctx.inp("w_W2L", W["W2L"])
    b1c = ctx.inp("w_b1c", W["b1c"])
    b2T = ctx.inp("w_b2T", W["b2T"])
    Wr = ctx.inp("w_Wr", W["Wr"])
    br = ctx.inp("w_br", W["br"])
    oh8 = ctx.inp("w_oh8", W["onehot8"])

    emit_ln_stats(ctx, "X")
    ctx.sb("lnb", (128, G * T), BF16)
    for g in range(G):
        emit_ln_apply_g(ctx, g, "X", S["lnb"][:, g * T:(g + 1) * T], "lnb")

    ctx.sb("wr_sb", (128, G * 8), F32)
    ctx.sb("wr_b", (128, G * 8), BF16)
    ctx.sb("br_sb", (8, 1), F32)
    ctx.sb("oh8f", (8, 8 * 128), F32)
    ctx.sb("oh8b", (8, 8 * 128), BF16)
    for g in range(G):
        ctx.load(S["wr_sb"][:, g * 8:(g + 1) * 8], Wr[g * 128:(g + 1) * 128, :],
                 wr=["wr_sb"])
    ctx.load(S["br_sb"][:], br[:, :], wr=["br_sb"])
    ctx.load(S["oh8f"][:], oh8[:, :], wr=["oh8f"])
    em.op("dve", lambda e: e.tensor_copy(S["wr_b"][:], S["wr_sb"][:]),
          reads=["wr_sb"], writes=["wr_b"])
    em.op("dve", lambda e: e.tensor_copy(S["oh8b"][:], S["oh8f"][:]),
          reads=["oh8f"], writes=["oh8b"])
    ctx.reg("logitsT", S["pf1"][0:8, :], "pf1")
    ctx.ps("p_lg", (8, CH), F32)
    for cc in range(NCH):
        for g in range(G):
            em.op("pe", lambda e, g=g, cc=cc: e.matmul(
                S["p_lg"][:], S["wr_b"][:, g * 8:(g + 1) * 8],
                S["lnb"][:, g * T + cc * CH:g * T + (cc + 1) * CH],
                start=(g == 0), stop=(g == G - 1)),
                reads=["wr_b", "lnb"], writes=["p_lg"])
        em.op("dve", lambda e, cc=cc: e.tensor_scalar(
            S["logitsT"][:, cc * CH:(cc + 1) * CH], S["p_lg"][:], S["br_sb"][:],
            None, AluOpType.add), reads=["p_lg", "br_sb"], writes=["logitsT"])

    ctx.reg("maskrows", S["pb2"][0:8, :], "pb2")
    ctx.reg("maskbc", S["pb3"], "pb3")
    ctx.sb("mxcol", (128, 1), F32)
    ctx.sb("masktm", (128, 8), F32)
    ctx.ps("p_t8", (128, 8), F32)
    ctx.ps("p_8t", (8, 128), F32)
    ctx.sb("ident8", (8, 8), F32)
    for tt in range(T // 128):
        em.op("pe", lambda e, tt=tt: e.transpose(
            S["p_t8"][:], S["logitsT"][:, tt * 128:(tt + 1) * 128],
            S["identf"][0:8, 0:8]), reads=["logitsT", "identf"], writes=["p_t8"])
        em.op("dve", lambda e: e.reduce_max(S["mxcol"][:], S["p_t8"][:], axis=AX.X),
              reads=["p_t8"], writes=["mxcol"])
        em.op("dve", lambda e: e.tensor_scalar(S["masktm"][:], S["p_t8"][:],
              S["mxcol"][:], None, AluOpType.is_ge),
              reads=["p_t8", "mxcol"], writes=["masktm"])
        em.op("pe", lambda e, tt=tt: e.transpose(
            S["p_8t"][:], S["masktm"][:], S["identf"][:]),
            reads=["masktm", "identf"], writes=["p_8t"])
        em.op("dve", lambda e, tt=tt: e.tensor_copy(
            S["maskrows"][:, tt * 128:(tt + 1) * 128], S["p_8t"][:]),
            reads=["p_8t"], writes=["maskrows"])

    ctx.sb("w1s", (128, D), F32)
    ctx.sb("w1b", (128, D), BF16)
    ctx.sb("w2s", (128, 2 * 1024), F32)
    ctx.sb("w2b", (128, 2 * 1024), BF16)
    ctx.sb("b1s", (128, 32), F32)
    ctx.sb("rb1", (128, 32), F32)
    ctx.sb("Mh", (128, 8 * T), BF16)
    ctx.sb("Rf", (128, CH), F32)
    ctx.ps("p_g0", (128, CH), F32)
    ctx.ps("p_w0", (128, CH), F32)
    ctx.ps("p_w1", (128, CH), F32)
    for e_i in range(8):
        for cc in range(NCH):
            em.op("pe", lambda e, e_i=e_i, cc=cc: e.matmul(
                S["plnb"][:], S["oh8b"][:, e_i * 128:(e_i + 1) * 128],
                S["maskrows"][:, cc * CH:(cc + 1) * CH], start=True, stop=True),
                reads=["maskrows", "oh8b"], writes=["plnb"])
            em.op("dve", lambda e, cc=cc: e.tensor_copy(
                S["maskbc"][:, cc * CH:(cc + 1) * CH], S["plnb"][:]),
                reads=["plnb"], writes=["maskbc"])
        ctx.load(S["b1s"][:], b1c[e_i, :, :], wr=["b1s"])
        em.op("act", lambda e: e.activation(S["rb1"][:], S["b1s"][:], AF.Relu),
              reads=["b1s"], writes=["rb1"])
        for half in range(4):
            for jj in range(8):
                j = half * 8 + jj
                ctx.load(S["w1s"][:], W1L[e_i, j, :, :], wr=["w1s"])
                em.op("dve", lambda e: e.tensor_copy(S["w1b"][:], S["w1s"][:]),
                      reads=["w1s"], writes=["w1b"])
                b1col = S["b1s"][:, j:j + 1]
                rb1col = S["rb1"][:, j:j + 1]
                for cc in range(NCH):
                    pg = "p_g0"
                    for g in range(G):
                        em.op("pe", lambda e, g=g, cc=cc, pg=pg: e.matmul(
                            S[pg][:], S["w1b"][:, g * 128:(g + 1) * 128],
                            S["lnb"][:, g * T + cc * CH:g * T + (cc + 1) * CH],
                            start=(g == 0), stop=(g == G - 1)),
                            reads=["w1b", "lnb"], writes=[pg])
                    em.op("act", lambda e, pg=pg, b1col=b1col: e.activation(
                        S["Rf"][:], S[pg][:], AF.Relu, bias=b1col),
                        reads=[pg, "b1s"], writes=["Rf"])
                    em.op("dve", lambda e, rb1col=rb1col: e.tensor_scalar(
                        S["Rf"][:], S["Rf"][:], rb1col, None, AluOpType.subtract),
                        reads=["Rf", "rb1"], writes=["Rf"])
                    Mc = S["Mh"][:, jj * T + cc * CH:jj * T + (cc + 1) * CH]
                    em.op("dve", lambda e, Mc=Mc, cc=cc: e.tensor_tensor(
                        Mc, S["Rf"][:], S["maskbc"][:, cc * CH:(cc + 1) * CH],
                        AluOpType.mult), reads=["Rf", "maskbc"], writes=["Mh"])
                    em.op("dve", lambda e, Mc=Mc, rb1col=rb1col: e.tensor_scalar(
                        Mc, Mc, rb1col, None, AluOpType.add),
                        reads=["Mh", "rb1"], writes=["Mh"])
            for dt_i in range(G):
                wsl = dt_i % 2
                ctx.load(S["w2s"][:, wsl * 1024:wsl * 1024 + 1024],
                         W2d[e_i, dt_i, :, half * 1024:(half + 1) * 1024], wr=[f"w2s{wsl}"])
                em.op("dve", lambda e, wsl=wsl: e.tensor_copy(
                    S["w2b"][:, wsl * 1024:wsl * 1024 + 1024],
                    S["w2s"][:, wsl * 1024:wsl * 1024 + 1024]),
                    reads=[f"w2s{wsl}"], writes=[f"w2b{wsl}"])
                for cc in range(NCH):
                    pw = f"p_w{cc % 2}"
                    for jj in range(8):
                        em.op("pe", lambda e, jj=jj, cc=cc, pw=pw, wsl=wsl: e.matmul(
                            S[pw][:], S["w2b"][:, wsl * 1024 + jj * 128:wsl * 1024 + (jj + 1) * 128],
                            S["Mh"][:, jj * T + cc * CH:jj * T + (cc + 1) * CH],
                            start=(jj == 0), stop=(jj == 7)),
                            reads=[f"w2b{wsl}", "Mh"], writes=[pw])
                    xg = S["X"][:, dt_i * T + cc * CH:dt_i * T + (cc + 1) * CH]
                    em.op("dve", lambda e, xg=xg, pw=pw: e.tensor_tensor(
                        xg, xg, S[pw][:], AluOpType.add),
                        reads=[pw, "X"], writes=["X"])
    ctx.sb("b2s", (128, 8), F32)
    ctx.sb("b2col", (128, 1), F32)
    for g in range(G):
        ctx.load(S["b2s"][:], b2T[g * 128:(g + 1) * 128, :], wr=["b2s"])
        em.op("dve", lambda e: e.reduce_sum(S["b2col"][:], S["b2s"][:], axis=AX.X),
              reads=["b2s"], writes=["b2col"])
        xg = S["X"][:, g * T:(g + 1) * T]
        em.op("dve", lambda e, xg=xg: e.tensor_scalar(xg, xg, S["b2col"][:],
              None, AluOpType.add), reads=["b2col", "X"], writes=["X"])


def build_stage2_program(W, consts, dump=()):
    ctx = Ctx("s2")
    ctx.consts = consts
    x_ap = ctx.inp("xdm", np.zeros((128, G * T), np.float32))
    o_ap = ctx.out("xdm_o", (128, G * T))
    ctx.sb("X", (128, G * T), F32)
    ctx.pools(3, 4)
    ctx_common(ctx, consts)
    ctx.load(ctx.S["X"][:], x_ap[:, :], wr=["X"])
    emit_stage2(ctx, W)
    tok = ctx.store(o_ap[:, :], ctx.S["X"][:], rd=["X"])
    emit_dumps(ctx, dump)
    ctx.em.wait("pool", [tok])
    ctx.em.build(ctx.stack)
    return ctx





def _tileL(w):
    return np.ascontiguousarray(
        w.reshape(8, 128, 8, 128).transpose(2, 1, 0, 3).reshape(8, 128, 1024))


def prep_weights3(inputs):
    W = {}
    W["WqTL"] = _tileL(inputs["L1_attn_Wq"].T)
    W["WkTL"] = _tileL(inputs["L1_attn_Wk"].T)
    W["WvTL"] = _tileL(inputs["L1_attn_Wv"].T)
    W["WoTL"] = _tileL(inputs["L1_attn_Wo"].T)
    W["bqc"] = np.ascontiguousarray(inputs["L1_attn_bq"].reshape(8, 128).T)
    W["bkc"] = np.ascontiguousarray(inputs["L1_attn_bk"].reshape(8, 128).T)
    W["bvc"] = np.ascontiguousarray(inputs["L1_attn_bv"].reshape(8, 128).T)
    W["boc"] = np.ascontiguousarray(inputs["L1_attn_bo"].reshape(8, 128).T)
    W["wd2"] = np.ascontiguousarray(np.concatenate([inputs["L1_attn_Wd"].T,
                                                    inputs["L1_attn_Wd"].T], 0))
    W["bd2"] = np.ascontiguousarray(np.concatenate([inputs["L1_attn_bd"],
                                                    inputs["L1_attn_bd"]]).reshape(128, 1))
    W["WgT"] = np.ascontiguousarray(inputs["L1_attn_Wg"].T)
    W["bg"] = inputs["L1_attn_bg"].reshape(64, 1)
    W["WgpT"] = np.ascontiguousarray(inputs["L1_attn_Wgp"].T)
    W["bgpc"] = np.ascontiguousarray(inputs["L1_attn_bgp"].reshape(8, 128).T)
    return W


def emit_stage3(ctx, W):
    em, S = ctx.em, ctx.S
    WqTL = ctx.inp("w_WqTL", W["WqTL"])
    WkTL = ctx.inp("w_WkTL", W["WkTL"])
    WvTL = ctx.inp("w_WvTL", W["WvTL"])
    WoTL = ctx.inp("w_WoTL", W["WoTL"])
    bqc = ctx.inp("w_bqc", W["bqc"])
    bkc = ctx.inp("w_bkc", W["bkc"])
    bvc = ctx.inp("w_bvc", W["bvc"])
    boc = ctx.inp("w_boc", W["boc"])
    wd2 = ctx.inp("w_wd2", W["wd2"])
    bd2 = ctx.inp("w_bd2", W["bd2"])
    WgT = ctx.inp("w_WgT", W["WgT"])
    bg = ctx.inp("w_bg", W["bg"])
    WgpT = ctx.inp("w_WgpT", W["WgpT"])
    bgpc = ctx.inp("w_bgpc", W["bgpc"])

    emit_ln_stats(ctx, "X")
    ctx.sb("lnb", (128, G * T), BF16)
    for g in range(G):
        emit_ln_apply_g(ctx, g, "X", S["lnb"][:, g * T:(g + 1) * T], "lnb")

    ctx.sb("biasc", (128, 4 * 8), F32)
    for i, ap in enumerate((bqc, bkc, bvc, boc)):
        ctx.load(S["biasc"][:, i * 8:(i + 1) * 8], ap[:, :], wr=["biasc"])
    ctx.sb("wd2f", (128, 64), F32)
    ctx.sb("wd2b", (128, 64), BF16)
    ctx.sb("bd2c", (128, 1), F32)
    ctx.load(S["wd2f"][:], wd2[:, :], wr=["wd2f"])
    ctx.load(S["bd2c"][:], bd2[:, :], wr=["bd2c"])
    em.op("dve", lambda e: e.tensor_copy(S["wd2b"][:], S["wd2f"][:]),
          reads=["wd2f"], writes=["wd2b"])
    ctx.sb("wgf", (128, G * 8), F32)
    ctx.sb("wgt_f", (128, G * 64), F32)
    ctx.sb("wgt_b", (128, G * 64), BF16)
    for g in range(G):
        ctx.load(S["wgt_f"][:, g * 64:(g + 1) * 64], WgT[g * 128:(g + 1) * 128, :],
                 wr=["wgt_f"])
    em.op("dve", lambda e: e.tensor_copy(S["wgt_b"][:], S["wgt_f"][:]),
          reads=["wgt_f"], writes=["wgt_b"])
    ctx.sb("wgp_f", (64, D), F32)
    ctx.sb("wgp_b", (64, D), BF16)
    ctx.load(S["wgp_f"][:], WgpT[:, :], wr=["wgp_f"])
    em.op("dve", lambda e: e.tensor_copy(S["wgp_b"][:], S["wgp_f"][:]),
          reads=["wgp_f"], writes=["wgp_b"])
    ctx.sb("bgcol", (64, 1), F32)
    ctx.load(S["bgcol"][:], bg[:, :], wr=["bgcol"])
    ctx.sb("bgpc_sb", (128, 8), F32)
    ctx.load(S["bgpc_sb"][:], bgpc[:, :], wr=["bgpc_sb"])

    ctx.sb("xmean", (128, 8), F32)
    ctx.sb("gatec", (128, 8), F32)
    ctx.sb("gsig", (64, 1), BF16)
    ctx.ps("p_d", (64, CH), F32)
    ctx.ps("p_q", (128, CH), F32)
    ctx.reg("p_go", S["p_q"][:, 0:1], "p_q")
    for g in range(G):
        em.op("dve", lambda e, g=g: e.reduce_sum(
            S["xmean"][:, g:g + 1], S["lnb"][:, g * T:(g + 1) * T], axis=AX.X),
            reads=["lnb"], writes=["xmean"])
    em.op("dve", lambda e: e.tensor_scalar_mul(S["xmean"][:], S["xmean"][:], 1.0 / T),
          reads=["xmean"], writes=["xmean"])
    ctx.sb("xmean_b", (128, 8), BF16)
    em.op("dve", lambda e: e.tensor_copy(S["xmean_b"][:], S["xmean"][:]),
          reads=["xmean"], writes=["xmean_b"])
    for g in range(G):
        em.op("pe", lambda e, g=g: e.matmul(
            S["p_d"][:, 0:1], S["wgt_b"][:, g * 64:(g + 1) * 64],
            S["xmean_b"][:, g:g + 1], start=(g == 0), stop=(g == G - 1)),
            reads=["wgt_b", "xmean_b"], writes=["p_d"])
    em.op("dve", lambda e: e.tensor_scalar(S["gsig"][:], S["p_d"][:, 0:1],
          S["bgcol"][:], None, AluOpType.add), reads=["p_d", "bgcol"], writes=["gsig"])
    em.op("act", lambda e: e.activation(S["gsig"][:], S["gsig"][:], AF.Sigmoid),
          reads=["gsig"], writes=["gsig"])
    for g in range(G):
        em.op("pe", lambda e, g=g: e.matmul(
            S["p_go"], S["wgp_b"][:, g * 128:(g + 1) * 128], S["gsig"][:],
            start=True, stop=True), reads=["wgp_b", "gsig"], writes=["p_go"])
        em.op("dve", lambda e, g=g: e.tensor_scalar(
            S["gatec"][:, g:g + 1], S["p_go"], S["bgpc_sb"][:, g:g + 1],
            None, AluOpType.add), reads=["p_go", "bgpc_sb"], writes=["gatec"])

    ctx.sb("AT", (128, G * T), BF16)
    ctx.reg("Qh", S["pb0"], "pb0")
    ctx.reg("Kh", S["pb1"], "pb1")
    ctx.reg("Vdh", S["pb2"], "pb2")
    ctx.reg("Vt", S["pb3"], "pb3")
    ctx.reg("Sb", S["pb4"], "pb4")
    ctx.reg("keep", S["pb5"], "pb5")
    ctx.reg("expS", S["pb6"], "pb6")
    ctx.reg("PT", S["pb7"], "pb7")
    ctx.reg("Sf", S["pf0"], "pf0")
    ctx.sb("wqs", (128, 2 * D), F32)
    ctx.sb("wqb", (128, 2 * D), BF16)
    ctx.sb("cols", (128, 16), F32)
    ctx.ps("p_s", (128, CH), F32)
    ctx.ps("p_t0", (128, 128), BF16)
    ctx.ps("p_t1", (128, 128), BF16)
    ctx.ps("p_o", (128, 128), F32)
    KSEL = 204
    lo = S["cols"][:, 0:1]; hi = S["cols"][:, 1:2]; mid = S["cols"][:, 2:3]
    cnt = S["cols"][:, 3:4]; pp = S["cols"][:, 4:5]; nM0 = S["cols"][:, 5:6]
    rsum = S["cols"][:, 6:7]; scr = S["cols"][:, 7:8]
    for h in range(G):
        for wi, (wl, bi, dstn) in enumerate(((WqTL, 0, "Qh"), (WkTL, 1, "Kh"),
                                             (WvTL, 2, "Vdh"))):
            wsl = wi % 2
            ctx.load(S["wqs"][:, wsl * D:wsl * D + D], wl[h, :, :], wr=[f"wqs{wsl}"])
            em.op("dve", lambda e, wsl=wsl: e.tensor_copy(
                S["wqb"][:, wsl * D:wsl * D + D], S["wqs"][:, wsl * D:wsl * D + D]),
                reads=[f"wqs{wsl}"], writes=[f"wqb{wsl}"])
            bcol = S["biasc"][:, bi * 8 + h:bi * 8 + h + 1]
            for cc in range(NCH):
                for g in range(G):
                    em.op("pe", lambda e, g=g, cc=cc, wsl=wsl: e.matmul(
                        S["p_q"][:], S["wqb"][:, wsl * D + g * 128:wsl * D + (g + 1) * 128],
                        S["lnb"][:, g * T + cc * CH:g * T + (cc + 1) * CH],
                        start=(g == 0), stop=(g == G - 1)),
                        reads=[f"wqb{wsl}", "lnb"], writes=["p_q"])
                em.op("dve", lambda e, dstn=dstn, cc=cc, bcol=bcol: e.tensor_scalar(
                    S[dstn][:, cc * CH:(cc + 1) * CH], S["p_q"][:], bcol, None,
                    AluOpType.add), reads=["p_q", "biasc"], writes=[dstn])
        for kb in range(16):
            pt = f"p_t{kb % 2}"
            em.op("pe", lambda e, kb=kb, pt=pt: e.transpose(
                S[pt][:], S["Vdh"][:, kb * 128:(kb + 1) * 128], S["identb_sb"][:]),
                reads=["Vdh", "identb_sb"], writes=[pt])
            em.op("dve", lambda e, kb=kb, pt=pt: e.tensor_copy(
                S["Vt"][:, kb * 128:(kb + 1) * 128], S[pt][:]),
                reads=[pt], writes=["Vt"])
        for qt in range(16):
            for cc in range(NCH):
                em.op("pe", lambda e, qt=qt, cc=cc: e.matmul(
                    S["p_s"][:], S["Qh"][:, qt * 128:(qt + 1) * 128],
                    S["Kh"][:, cc * CH:(cc + 1) * CH], start=True, stop=True),
                    reads=["Qh", "Kh"], writes=["p_s"])
                em.op("dve", lambda e, cc=cc: e.tensor_scalar_mul(
                    S["Sf"][:, cc * CH:(cc + 1) * CH], S["p_s"][:], 0.08838834764831845),
                    reads=["p_s"], writes=["Sf"])
            em.op("dve", lambda e: e.tensor_copy(S["Sb"][:], S["Sf"][:]),
                  reads=["Sf"], writes=["Sb"])
            em.op("dve", lambda e: e.reduce_max(nM0, S["Sf"][:], axis=AX.X,
                  negate=True), reads=["Sf"], writes=["cols"])
            em.op("dve", lambda e: e.tensor_scalar_mul(hi, nM0, -1.0),
                  reads=["cols"], writes=["cols"])
            em.op("dve", lambda e: e.tensor_reduce(lo, S["Sb"][:], AX.X,
                  AluOpType.min), reads=["Sb"], writes=["cols"])
            for it in range(12):
                em.op("dve", lambda e: e.tensor_tensor(mid, lo, hi, AluOpType.add),
                      reads=["cols"], writes=["cols"])
                em.op("dve", lambda e: e.tensor_scalar_mul(mid, mid, 0.5),
                      reads=["cols"], writes=["cols"])
                em.op("dve", lambda e: e.tensor_scalar(S["keep"][:], S["Sb"][:],
                      mid, None, AluOpType.is_ge), reads=["Sb", "cols"], writes=["keep"])
                em.op("dve", lambda e: e.tensor_reduce(cnt, S["keep"][:], AX.X,
                      AluOpType.add), reads=["keep"], writes=["cols"])
                em.op("dve", lambda e: e.tensor_scalar(pp, cnt, float(KSEL), None,
                      AluOpType.is_ge), reads=["cols"], writes=["cols"])
                em.op("dve", lambda e: e.tensor_tensor(scr, mid, lo, AluOpType.subtract),
                      reads=["cols"], writes=["cols"])
                em.op("dve", lambda e: e.tensor_tensor(scr, scr, pp, AluOpType.mult),
                      reads=["cols"], writes=["cols"])
                em.op("dve", lambda e: e.tensor_tensor(lo, lo, scr, AluOpType.add),
                      reads=["cols"], writes=["cols"])
                em.op("dve", lambda e: e.tensor_tensor(scr, hi, mid, AluOpType.subtract),
                      reads=["cols"], writes=["cols"])
                em.op("dve", lambda e: e.tensor_tensor(scr, scr, pp, AluOpType.mult),
                      reads=["cols"], writes=["cols"])
                em.op("dve", lambda e: e.tensor_tensor(hi, hi, scr, AluOpType.subtract),
                      reads=["cols"], writes=["cols"])
            em.op("dve", lambda e: e.tensor_scalar(S["keep"][:], S["Sb"][:], lo,
                  None, AluOpType.is_ge), reads=["Sb", "cols"], writes=["keep"])
            em.op("act", lambda e: e.activation(S["expS"][:], S["Sf"][:], AF.Exp,
                  bias=nM0), reads=["Sf", "cols"], writes=["expS"])
            em.op("dve", lambda e: e.scalar_tensor_tensor(S["expS"][:], S["expS"][:],
                  1.0, S["keep"][:], AluOpType.mult, AluOpType.mult, accum_out=rsum),
                  reads=["expS", "keep"], writes=["expS", "cols"])
            em.op("dve", lambda e: e.reciprocal(rsum, rsum), reads=["cols"], writes=["cols"])
            em.op("dve", lambda e: e.tensor_scalar(S["expS"][:], S["expS"][:], rsum,
                  None, AluOpType.mult), reads=["expS", "cols"], writes=["expS"])
            for kb in range(16):
                pt = f"p_t{kb % 2}"
                em.op("pe", lambda e, kb=kb, pt=pt: e.transpose(
                    S[pt][:], S["expS"][:, kb * 128:(kb + 1) * 128], S["identb_sb"][:]),
                    reads=["expS", "identb_sb"], writes=[pt])
                em.op("dve", lambda e, kb=kb, pt=pt: e.tensor_copy(
                    S["PT"][:, kb * 128:(kb + 1) * 128], S[pt][:]),
                    reads=[pt], writes=["PT"])
            for kb in range(16):
                em.op("pe", lambda e, kb=kb: e.matmul(
                    S["p_o"][:], S["Vt"][:, kb * 128:(kb + 1) * 128],
                    S["PT"][:, kb * 128:(kb + 1) * 128],
                    start=(kb == 0), stop=(kb == 15)),
                    reads=["Vt", "PT"], writes=["p_o"])
            em.op("dve", lambda e, h=h, qt=qt: e.tensor_copy(
                S["AT"][:, h * T + qt * 128:h * T + (qt + 1) * 128], S["p_o"][:]),
                reads=["p_o"], writes=["AT"])

    ctx.reg("dist", S["pf2"], "pf2")
    for g in range(G):
        for sub in range(2):
            for cc in range(NCH):
                em.op("pe", lambda e, g=g, sub=sub, cc=cc: e.matmul(
                    S["p_d"][:], S["wd2b"][sub * 64:sub * 64 + 64, :],
                    S["AT"][sub * 64:sub * 64 + 64,
                            g * T + cc * CH:g * T + (cc + 1) * CH],
                    start=True, stop=True), reads=["wd2b", "AT"], writes=["p_d"])
                em.op("dve", lambda e, g=g, sub=sub, cc=cc: e.tensor_scalar(
                    S["dist"][sub * 64:sub * 64 + 64, cc * CH:(cc + 1) * CH],
                    S["p_d"][:], S["bd2c"][sub * 64:sub * 64 + 64, :], None,
                    AluOpType.add), reads=["p_d", "bd2c"], writes=["dist"])
        atg = S["AT"][:, g * T:(g + 1) * T]
        em.op("dve", lambda e, atg=atg: e.tensor_tensor(
            S["dist"][:], S["dist"][:], atg, AluOpType.subtract),
            reads=["dist", "AT"], writes=["dist"])
        em.op("dve", lambda e, g=g: e.tensor_scalar(
            S["dist"][:], S["dist"][:], S["gatec"][:, g:g + 1], None,
            AluOpType.mult), reads=["dist", "gatec"], writes=["dist"])
        em.op("dve", lambda e, atg=atg: e.tensor_tensor(
            atg, atg, S["dist"][:], AluOpType.add), reads=["dist", "AT"], writes=["AT"])

    for dq in range(G):
        wsl = dq % 2
        ctx.load(S["wqs"][:, wsl * D:wsl * D + D], WoTL[dq, :, :], wr=[f"wqs{wsl}"])
        em.op("dve", lambda e, wsl=wsl: e.tensor_copy(
            S["wqb"][:, wsl * D:wsl * D + D], S["wqs"][:, wsl * D:wsl * D + D]),
            reads=[f"wqs{wsl}"], writes=[f"wqb{wsl}"])
        bocol = S["biasc"][:, 3 * 8 + dq:3 * 8 + dq + 1]
        for cc in range(NCH):
            for g in range(G):
                em.op("pe", lambda e, g=g, cc=cc, wsl=wsl: e.matmul(
                    S["p_q"][:], S["wqb"][:, wsl * D + g * 128:wsl * D + (g + 1) * 128],
                    S["AT"][:, g * T + cc * CH:g * T + (cc + 1) * CH],
                    start=(g == 0), stop=(g == G - 1)),
                    reads=[f"wqb{wsl}", "AT"], writes=["p_q"])
            xg = S["X"][:, dq * T + cc * CH:dq * T + (cc + 1) * CH]
            em.op("dve", lambda e, xg=xg: e.tensor_tensor(
                xg, xg, S["p_q"][:], AluOpType.add), reads=["p_q", "X"], writes=["X"])
            em.op("dve", lambda e, xg=xg, bocol=bocol: e.tensor_scalar(
                xg, xg, bocol, None, AluOpType.add), reads=["biasc", "X"], writes=["X"])


def build_stage3_program(W, consts, dump=()):
    ctx = Ctx("s3")
    ctx.consts = consts
    x_ap = ctx.inp("xdm", np.zeros((128, G * T), np.float32))
    o_ap = ctx.out("xdm_o", (128, G * T))
    ctx.sb("X", (128, G * T), F32)
    ctx.pools(3, 8)
    ctx_common(ctx, consts)
    ctx.sb("identb_sb", (128, 128), BF16)
    ctx.em.op("dve", lambda e: e.tensor_copy(ctx.S["identb_sb"][:], ctx.S["identf"][:]),
              reads=["identf"], writes=["identb_sb"])
    ctx.load(ctx.S["X"][:], x_ap[:, :], wr=["X"])
    emit_stage3(ctx, W)
    tok = ctx.store(o_ap[:, :], ctx.S["X"][:], rd=["X"])
    emit_dumps(ctx, dump)
    ctx.em.wait("pool", [tok])
    ctx.em.build(ctx.stack)
    return ctx



def prep_weights4(inputs):
    W = {}
    aw = inputs["L1_ffn_attn_w"]
    W["awL"] = np.ascontiguousarray(
        aw.reshape(8, 128, 48, 64).transpose(2, 1, 0, 3).reshape(48, 128, 512))
    W["projL"] = _tileL(inputs["L1_ffn_proj_w"])
    W["l1w"] = np.ascontiguousarray(inputs["L1_ffn_ln1_w"].reshape(8, 128).T)
    W["l1b"] = np.ascontiguousarray(inputs["L1_ffn_ln1_b"].reshape(8, 128).T)
    W["cvw"] = np.ascontiguousarray(
        inputs["L1_ffn_conv_w"].reshape(8, 128, 3).transpose(1, 0, 2).reshape(128, 24))
    return W


def emit_stage4(ctx, W, consts):
    em, S = ctx.em, ctx.S
    awL = ctx.inp("w_awL", W["awL"])
    projL = ctx.inp("w_projL", W["projL"])
    l1w = ctx.inp("w_l1w", W["l1w"])
    l1b = ctx.inp("w_l1b", W["l1b"])
    cvw = ctx.inp("w_cvw", W["cvw"])
    ropeC = ctx.inp("c_ropeCos", consts["ropeCos"], dt=BF16)
    ropeS = ctx.inp("c_ropeSin", consts["ropeSin"], dt=BF16)
    cmask = ctx.inp("c_cmask", consts["cmask"].reshape(4 * 128, 512), dt=BF16)

    ctx.sb("l1w_sb", (128, 8), F32)
    ctx.sb("l1b_sb", (128, 8), F32)
    ctx.sb("cvw_sb", (128, 24), F32)
    ctx.load(S["l1w_sb"][:], l1w[:, :], wr=["l1w_sb"])
    ctx.load(S["l1b_sb"][:], l1b[:, :], wr=["l1b_sb"])
    ctx.load(S["cvw_sb"][:], cvw[:, :], wr=["cvw_sb"])
    ctx.sb("ropeC_sb", (64, T), BF16)
    ctx.sb("ropeS_sb", (64, T), BF16)
    ctx.load(S["ropeC_sb"][:], ropeC[:, :], wr=["ropeC_sb"])
    ctx.load(S["ropeS_sb"][:], ropeS[:, :], wr=["ropeS_sb"])
    ctx.sb("cmask_sb", (128, 4 * 512), BF16)
    for i in range(4):
        ctx.load(S["cmask_sb"][:, i * 512:(i + 1) * 512],
                 cmask[i * 128:(i + 1) * 128, :], wr=["cmask_sb"])

    emit_ln_stats(ctx, "X")
    for g in range(G):
        xg = S["X"][:, g * T:(g + 1) * T]
        em.op("dve", lambda e, xg=xg: e.tensor_tensor(xg, xg, S["m_bc"][:],
              AluOpType.subtract), reads=["m_bc", "X"], writes=["X"])
        em.op("dve", lambda e, xg=xg: e.tensor_tensor(xg, xg, S["rs_bc"][:],
              AluOpType.mult), reads=["rs_bc", "X"], writes=["X"])
    emit_ln_stats(ctx, "X")
    ctx.sb("lnb", (128, G * T), BF16)
    for g in range(G):
        emit_ln_apply_g(ctx, g, "X", S["lnb"][:, g * T:(g + 1) * T], "lnb",
                        w_col=S["l1w_sb"][:, g:g + 1], b_col=S["l1b_sb"][:, g:g + 1],
                        w_buf=["l1w_sb", "l1b_sb"])
    ctx.sb("hc", (128, G * T), BF16)
    for g in range(G):
        hg = S["lnb"][:, g * T:(g + 1) * T]
        og = S["hc"][:, g * T:(g + 1) * T]
        w0 = S["cvw_sb"][:, g * 3:g * 3 + 1]
        w1 = S["cvw_sb"][:, g * 3 + 1:g * 3 + 2]
        w2 = S["cvw_sb"][:, g * 3 + 2:g * 3 + 3]
        em.op("dve", lambda e, hg=hg, og=og, w2=w2: e.tensor_scalar(
            og, hg, w2, None, AluOpType.mult), reads=["lnb", "cvw_sb"], writes=["hc"])
        em.op("dve", lambda e, hg=hg, w1=w1: e.tensor_scalar(
            S["lntmp"][:, 0:T - 1], hg[:, 0:T - 1], w1, None, AluOpType.mult),
            reads=["lnb", "cvw_sb"], writes=["lntmp"])
        em.op("dve", lambda e, og=og: e.tensor_tensor(
            og[:, 1:T], og[:, 1:T], S["lntmp"][:, 0:T - 1], AluOpType.add),
            reads=["lntmp", "hc"], writes=["hc"])
        em.op("dve", lambda e, hg=hg, w0=w0: e.tensor_scalar(
            S["lntmp"][:, 0:T - 2], hg[:, 0:T - 2], w0, None, AluOpType.mult),
            reads=["lnb", "cvw_sb"], writes=["lntmp"])
        em.op("dve", lambda e, og=og: e.tensor_tensor(
            og[:, 2:T], og[:, 2:T], S["lntmp"][:, 0:T - 2], AluOpType.add),
            reads=["lntmp", "hc"], writes=["hc"])

    ctx.reg("Qh", S["pb0"][0:64, :], "pb0")
    ctx.reg("Kh", S["pb1"][0:64, :], "pb1")
    ctx.reg("Vdh", S["pb2"][0:64, :], "pb2")
    ctx.reg("Vt", S["pb3"], "pb3")
    ctx.reg("expS", S["pb4"], "pb4")
    ctx.reg("PT", S["pb5"], "pb5")
    ctx.reg("Sf", S["pf0"], "pf0")
    ctx.reg("qf", S["pf2"][0:64, 0:T], "pf2")
    ctx.reg("rf", S["pf1"][0:64, 0:T], "pf1")
    ctx.sb("aws", (128, 2 * D), F32)
    ctx.sb("awb", (128, 2 * D), BF16)
    ctx.sb("cols4", (128, 8), F32)
    nM0 = S["cols4"][:, 0:1]
    rsum = S["cols4"][:, 1:2]
    ctx.ps("p_q", (64, CH), F32)
    ctx.ps("p_s", (128, CH), F32)
    ctx.ps("p_t0", (128, 128), BF16)
    ctx.ps("p_t1", (128, 128), BF16)
    ctx.ps("p_o", (64, 128), F32)
    for h in range(16):
        for wi, dstn in ((0, "Qh"), (1, "Kh"), (2, "Vdh")):
            wsl = wi % 2
            blk = wi * 16 + h
            ctx.load(S["aws"][:, wsl * D:wsl * D + 512], awL[blk, :, :],
                     wr=[f"aws{wsl}"])
            em.op("dve", lambda e, wsl=wsl: e.tensor_copy(
                S["awb"][:, wsl * D:wsl * D + 512],
                S["aws"][:, wsl * D:wsl * D + 512]),
                reads=[f"aws{wsl}"], writes=[f"awb{wsl}"])
            for cc in range(NCH):
                for g in range(G):
                    em.op("pe", lambda e, g=g, cc=cc, wsl=wsl: e.matmul(
                        S["p_q"][:], S["awb"][:, wsl * D + g * 64:wsl * D + (g + 1) * 64],
                        S["hc"][:, g * T + cc * CH:g * T + (cc + 1) * CH],
                        start=(g == 0), stop=(g == G - 1)),
                        reads=[f"awb{wsl}", "hc"], writes=["p_q"])
                if dstn == "Vdh":
                    em.op("dve", lambda e, cc=cc: e.tensor_copy(
                        S["Vdh"][:, cc * CH:(cc + 1) * CH], S["p_q"][:]),
                        reads=["p_q"], writes=["pb2"])
                else:
                    em.op("dve", lambda e, cc=cc: e.tensor_copy(
                        S["qf"][:, cc * CH:(cc + 1) * CH], S["p_q"][:]),
                        reads=["p_q"], writes=["pf2"])
            if dstn != "Vdh":
                em.op("dve", lambda e: e.tensor_scalar_mul(
                    S["rf"][0:32, :], S["qf"][32:64, :], -1.0),
                    reads=["pf2"], writes=["pf1"])
                em.op("dve", lambda e: e.tensor_copy(
                    S["rf"][32:64, :], S["qf"][0:32, :]),
                    reads=["pf2"], writes=["pf1"])
                em.op("dve", lambda e: e.tensor_tensor(
                    S["rf"], S["rf"], S["ropeS_sb"][:], AluOpType.mult),
                    reads=["pf1", "ropeS_sb"], writes=["pf1"])
                em.op("dve", lambda e: e.tensor_tensor(
                    S["qf"], S["qf"], S["ropeC_sb"][:], AluOpType.mult),
                    reads=["pf2", "ropeC_sb"], writes=["pf2"])
                em.op("dve", lambda e, dstn=dstn: e.tensor_tensor(
                    S[dstn], S["qf"], S["rf"], AluOpType.add),
                    reads=["pf1", "pf2"], writes=[("pb0" if dstn == "Qh" else "pb1")])
        for kb in range(16):
            pt = f"p_t{kb % 2}"
            em.op("pe", lambda e, kb=kb, pt=pt: e.transpose(
                S[pt][0:128, 0:64], S["Vdh"][:, kb * 128:(kb + 1) * 128],
                S["identb_sb"][0:64, 0:64]), reads=["pb2", "identb_sb"], writes=[pt])
            em.op("dve", lambda e, kb=kb, pt=pt: e.tensor_copy(
                S["Vt"][:, kb * 64:(kb + 1) * 64], S[pt][0:128, 0:64]),
                reads=[pt], writes=["pb3"])
        for qt in range(16):
            ncc = qt // 4 + 1
            for cc in range(ncc):
                em.op("pe", lambda e, qt=qt, cc=cc: e.matmul(
                    S["p_s"][:], S["Qh"][:, qt * 128:(qt + 1) * 128],
                    S["Kh"][:, cc * CH:(cc + 1) * CH], start=True, stop=True),
                    reads=["pb0", "pb1"], writes=["p_s"])
                em.op("dve", lambda e, cc=cc: e.tensor_scalar_mul(
                    S["Sf"][:, cc * CH:(cc + 1) * CH], S["p_s"][:], 0.125),
                    reads=["p_s"], writes=["pf0"])
            moff = qt % 4
            em.op("dve", lambda e, moff=moff, ncc=ncc: e.tensor_tensor(
                S["Sf"][:, (ncc - 1) * CH:ncc * CH],
                S["Sf"][:, (ncc - 1) * CH:ncc * CH],
                S["cmask_sb"][:, moff * 512:(moff + 1) * 512], AluOpType.add),
                reads=["cmask_sb", "pf0"], writes=["pf0"])
            WW = ncc * CH
            em.op("dve", lambda e, WW=WW: e.reduce_max(nM0, S["Sf"][:, 0:WW],
                  axis=AX.X, negate=True), reads=["pf0"], writes=["cols4"])
            em.op("act", lambda e, WW=WW: e.activation(S["expS"][:, 0:WW],
                  S["Sf"][:, 0:WW], AF.Exp, bias=nM0, accum_out=rsum),
                  reads=["pf0", "cols4"], writes=["pb4", "cols4"])
            em.op("dve", lambda e: e.reciprocal(rsum, rsum),
                  reads=["cols4"], writes=["cols4"])
            em.op("dve", lambda e, WW=WW: e.tensor_scalar(S["expS"][:, 0:WW],
                  S["expS"][:, 0:WW], rsum, None, AluOpType.mult),
                  reads=["pb4", "cols4"], writes=["pb4"])
            nkb = qt + 1
            for kb in range(nkb):
                pt = f"p_t{kb % 2}"
                em.op("pe", lambda e, kb=kb, pt=pt: e.transpose(
                    S[pt][:], S["expS"][:, kb * 128:(kb + 1) * 128],
                    S["identb_sb"][:]), reads=["pb4", "identb_sb"], writes=[pt])
                em.op("dve", lambda e, kb=kb, pt=pt: e.tensor_copy(
                    S["PT"][:, kb * 128:(kb + 1) * 128], S[pt][:]),
                    reads=[pt], writes=["pb5"])
            for kb in range(nkb):
                em.op("pe", lambda e, kb=kb: e.matmul(
                    S["p_o"][:], S["Vt"][kb * 128 - kb * 128:128,
                                         kb * 64:(kb + 1) * 64]
                    if False else S["Vt"][:, kb * 64:(kb + 1) * 64],
                    S["PT"][:, kb * 128:(kb + 1) * 128],
                    start=(kb == 0), stop=(kb == nkb - 1)),
                    reads=["pb3", "pb5"], writes=["p_o"])
            dst = S["hc"][(h % 2) * 64:(h % 2) * 64 + 64,
                          (h // 2) * T + qt * 128:(h // 2) * T + (qt + 1) * 128]
            em.op("dve", lambda e, dst=dst: e.tensor_copy(dst, S["p_o"][:]),
                  reads=["p_o"], writes=["hc"])
    ctx.reg("wqs", S["aws"], "aws_all")
    ctx.reg("wqb", S["awb"], "awb_all")
    for old, newn in (("aws0", "aws_all"), ("aws1", "aws_all"),
                      ("awb0", "awb_all"), ("awb1", "awb_all")):
        ob = ctx.em.bufs.get(old)
        nb = ctx.em.buf(newn)
        if ob is not None:
            if ob.ready is not None:
                nb.readers.append(ob.ready)
            nb.readers.extend(ob.readers)
        ctx.em.alias[old] = newn
    ctx.ps("p_pr", (128, CH), F32)
    for dq in range(G):
        wsl = dq % 2
        ctx.load(S["wqs"][:, wsl * D:wsl * D + D], projL[dq, :, :], wr=["aws_all"])
        em.op("dve", lambda e, wsl=wsl: e.tensor_copy(
            S["wqb"][:, wsl * D:wsl * D + D], S["wqs"][:, wsl * D:wsl * D + D]),
            reads=["aws_all"], writes=["awb_all"])
        for cc in range(NCH):
            for g in range(G):
                em.op("pe", lambda e, g=g, cc=cc, wsl=wsl: e.matmul(
                    S["p_pr"][:], S["wqb"][:, wsl * D + g * 128:wsl * D + (g + 1) * 128],
                    S["hc"][:, g * T + cc * CH:g * T + (cc + 1) * CH],
                    start=(g == 0), stop=(g == G - 1)),
                    reads=["awb_all", "hc"], writes=["p_pr"])
            xg = S["X"][:, dq * T + cc * CH:dq * T + (cc + 1) * CH]
            em.op("dve", lambda e, xg=xg: e.tensor_tensor(
                xg, xg, S["p_pr"][:], AluOpType.add), reads=["p_pr", "X"], writes=["X"])


def build_stage4_program(W, consts, dump=()):
    ctx = Ctx("s4")
    ctx.consts = consts
    x_ap = ctx.inp("xdm", np.zeros((128, G * T), np.float32))
    o_ap = ctx.out("zdm", (128, G * T))
    ctx.sb("X", (128, G * T), F32)
    ctx.pools(3, 6)
    ctx_common(ctx, consts)
    ctx.sb("identb_sb", (128, 128), BF16)
    ctx.em.op("dve", lambda e: e.tensor_copy(ctx.S["identb_sb"][:], ctx.S["identf"][:]),
              reads=["identf"], writes=["identb_sb"])
    ctx.load(ctx.S["X"][:], x_ap[:, :], wr=["X"])
    emit_stage4(ctx, W, consts)
    tok = ctx.store(o_ap[:, :], ctx.S["X"][:], rd=["X"])
    emit_dumps(ctx, dump)
    ctx.em.wait("pool", [tok])
    ctx.em.build(ctx.stack)
    return ctx



def prep_weights5(inputs):
    W = {}
    wv = inputs["L1_ffn_expert_wv"]
    W["WvL"] = np.ascontiguousarray(
        wv.reshape(4, 8, 128, 64, 128).transpose(0, 3, 2, 1, 4).reshape(4, 64, 128, 1024))
    W["PrL"] = np.ascontiguousarray(
        inputs["L1_ffn_expert_proj"].reshape(4, 32, 128, 8, 128)
        .transpose(0, 3, 2, 1, 4).reshape(4, 8, 128, 4096))
    W["l2w"] = np.ascontiguousarray(inputs["L1_ffn_ln2_w"].reshape(8, 128).T)
    W["l2b"] = np.ascontiguousarray(inputs["L1_ffn_ln2_b"].reshape(8, 128).T)
    W["router"] = inputs["L1_ffn_router_w"]
    oh = np.zeros((4, 4 * 128), np.float32)
    for e in range(4):
        oh[e, e * 128:(e + 1) * 128] = 1.0
    W["onehot4"] = oh
    return W


def emit_stage5(ctx, W, x3_ap):
    em, S = ctx.em, ctx.S
    WvL = ctx.inp("w_WvL", W["WvL"])
    PrL = ctx.inp("w_PrL", W["PrL"])
    l2w = ctx.inp("w_l2w", W["l2w"])
    l2b = ctx.inp("w_l2b", W["l2b"])
    router = ctx.inp("w_router", W["router"])
    oh4 = ctx.inp("w_oh4", W["onehot4"])

    ctx.sb("l2w_sb", (128, 8), F32)
    ctx.sb("l2b_sb", (128, 8), F32)
    ctx.load(S["l2w_sb"][:], l2w[:, :], wr=["l2w_sb"])
    ctx.load(S["l2b_sb"][:], l2b[:, :], wr=["l2b_sb"])

    emit_ln_stats(ctx, "X")
    ctx.sb("lnb", (128, G * T), BF16)
    for g in range(G):
        emit_ln_apply_g(ctx, g, "X", S["lnb"][:, g * T:(g + 1) * T], "lnb",
                        w_col=S["l2w_sb"][:, g:g + 1], b_col=S["l2b_sb"][:, g:g + 1],
                        w_buf=["l2w_sb", "l2b_sb"])

    ctx.sb("rt_sb", (128, G * 4), F32)
    ctx.sb("rt_b", (128, G * 4), BF16)
    ctx.sb("oh4f", (4, 4 * 128), F32)
    ctx.sb("oh4b", (4, 4 * 128), BF16)
    for g in range(G):
        ctx.load(S["rt_sb"][:, g * 4:(g + 1) * 4], router[g * 128:(g + 1) * 128, :],
                 wr=["rt_sb"])
    ctx.load(S["oh4f"][:], oh4[:, :], wr=["oh4f"])
    em.op("dve", lambda e: e.tensor_copy(S["rt_b"][:], S["rt_sb"][:]),
          reads=["rt_sb"], writes=["rt_b"])
    em.op("dve", lambda e: e.tensor_copy(S["oh4b"][:], S["oh4f"][:]),
          reads=["oh4f"], writes=["oh4b"])
    ctx.reg("logitsT", S["pf1"][0:4, :], "pf1")
    ctx.ps("p_lg", (4, CH), F32)
    for cc in range(NCH):
        for g in range(G):
            em.op("pe", lambda e, g=g, cc=cc: e.matmul(
                S["p_lg"][:], S["rt_b"][:, g * 4:(g + 1) * 4],
                S["lnb"][:, g * T + cc * CH:g * T + (cc + 1) * CH],
                start=(g == 0), stop=(g == G - 1)),
                reads=["rt_b", "lnb"], writes=["p_lg"])
        em.op("dve", lambda e, cc=cc: e.tensor_copy(
            S["logitsT"][:, cc * CH:(cc + 1) * CH], S["p_lg"][:]),
            reads=["p_lg"], writes=["logitsT"])
    ctx.reg("maskrows", S["pb2"][0:4, :], "pb2")
    ctx.reg("maskbc", S["pb3"], "pb3")
    ctx.sb("mxcol", (128, 1), F32)
    ctx.sb("masktm", (128, 4), F32)
    ctx.ps("p_t8", (128, 4), F32)
    ctx.ps("p_8t", (4, 128), F32)
    for tt in range(T // 128):
        em.op("pe", lambda e, tt=tt: e.transpose(
            S["p_t8"][:], S["logitsT"][:, tt * 128:(tt + 1) * 128],
            S["identf"][0:4, 0:4]), reads=["logitsT", "identf"], writes=["p_t8"])
        em.op("dve", lambda e: e.reduce_max(S["mxcol"][:], S["p_t8"][:], axis=AX.X),
              reads=["p_t8"], writes=["mxcol"])
        em.op("dve", lambda e: e.tensor_scalar(S["masktm"][:], S["p_t8"][:],
              S["mxcol"][:], None, AluOpType.is_ge),
              reads=["p_t8", "mxcol"], writes=["masktm"])
        em.op("pe", lambda e, tt=tt: e.transpose(
            S["p_8t"][:], S["masktm"][:], S["identf"][:]),
            reads=["masktm", "identf"], writes=["p_8t"])
        em.op("dve", lambda e, tt=tt: e.tensor_copy(
            S["maskrows"][:, tt * 128:(tt + 1) * 128], S["p_8t"][:]),
            reads=["p_8t"], writes=["maskrows"])

    ctx.sb("wvs", (128, 2 * D), F32)
    ctx.sb("wvb", (128, 2 * D), BF16)
    ctx.sb("prs", (128, D), F32)
    ctx.sb("prb", (128, D), BF16)
    ctx.sb("Mh", (128, 8 * T), BF16)
    ctx.sb("sg", (128, CH), BF16)
    ctx.ps("p_g0", (128, CH), F32)
    ctx.ps("p_w0", (128, CH), F32)
    ctx.ps("p_w1", (128, CH), F32)
    for e_i in range(4):
        for cc in range(NCH):
            em.op("pe", lambda e, e_i=e_i, cc=cc: e.matmul(
                S["plnb"][:], S["oh4b"][:, e_i * 128:(e_i + 1) * 128],
                S["maskrows"][:, cc * CH:(cc + 1) * CH], start=True, stop=True),
                reads=["maskrows", "oh4b"], writes=["plnb"])
            em.op("dve", lambda e, cc=cc: e.tensor_copy(
                S["maskbc"][:, cc * CH:(cc + 1) * CH], S["plnb"][:]),
                reads=["plnb"], writes=["maskbc"])
        for qg in range(4):
            for jj in range(8):
                j = qg * 8 + jj
                for which, wsl in ((j, 0), (j + 32, 1)):
                    ctx.load(S["wvs"][:, wsl * D:wsl * D + D], WvL[e_i, which, :, :],
                             wr=[f"wvs{wsl}"])
                    em.op("dve", lambda e, wsl=wsl: e.tensor_copy(
                        S["wvb"][:, wsl * D:wsl * D + D],
                        S["wvs"][:, wsl * D:wsl * D + D]),
                        reads=[f"wvs{wsl}"], writes=[f"wvb{wsl}"])
                for cc in range(NCH):
                    for g in range(G):
                        em.op("pe", lambda e, g=g, cc=cc: e.matmul(
                            S["p_g0"][:], S["wvb"][:, g * 128:(g + 1) * 128],
                            S["lnb"][:, g * T + cc * CH:g * T + (cc + 1) * CH],
                            start=(g == 0), stop=(g == G - 1)),
                            reads=["wvb0", "lnb"], writes=["p_g0"])
                    em.op("act", lambda e: e.activation(S["sg"][:], S["p_g0"][:],
                          AF.Silu), reads=["p_g0"], writes=["sg"])
                    for g in range(G):
                        em.op("pe", lambda e, g=g, cc=cc: e.matmul(
                            S["p_g0"][:], S["wvb"][:, D + g * 128:D + (g + 1) * 128],
                            S["lnb"][:, g * T + cc * CH:g * T + (cc + 1) * CH],
                            start=(g == 0), stop=(g == G - 1)),
                            reads=["wvb1", "lnb"], writes=["p_g0"])
                    Mc = S["Mh"][:, jj * T + cc * CH:jj * T + (cc + 1) * CH]
                    em.op("dve", lambda e, Mc=Mc: e.tensor_tensor(
                        Mc, S["sg"][:], S["p_g0"][:], AluOpType.mult),
                        reads=["sg", "p_g0"], writes=["Mh"])
                    em.op("dve", lambda e, Mc=Mc, cc=cc: e.tensor_tensor(
                        Mc, Mc, S["maskbc"][:, cc * CH:(cc + 1) * CH],
                        AluOpType.mult), reads=["maskbc", "Mh"], writes=["Mh"])
            for dt_i in range(G):
                ctx.load(S["prs"][:], PrL[e_i, dt_i, :, qg * 1024:(qg + 1) * 1024],
                         wr=["prs"])
                em.op("dve", lambda e: e.tensor_copy(S["prb"][:], S["prs"][:]),
                      reads=["prs"], writes=["prb"])
                for cc in range(NCH):
                    pw = f"p_w{cc % 2}"
                    for jj in range(8):
                        em.op("pe", lambda e, jj=jj, cc=cc, pw=pw: e.matmul(
                            S[pw][:], S["prb"][:, jj * 128:(jj + 1) * 128],
                            S["Mh"][:, jj * T + cc * CH:jj * T + (cc + 1) * CH],
                            start=(jj == 0), stop=(jj == 7)),
                            reads=["prb", "Mh"], writes=[pw])
                    xg = S["X"][:, dt_i * T + cc * CH:dt_i * T + (cc + 1) * CH]
                    em.op("dve", lambda e, xg=xg, pw=pw: e.tensor_tensor(
                        xg, xg, S[pw][:], AluOpType.add),
                        reads=[pw, "X"], writes=["X"])
    for g in range(G):
        ctx.load(S["pf0"][:], x3_ap[:, g * T:(g + 1) * T], wr=["pf0"])
        xg = S["X"][:, g * T:(g + 1) * T]
        em.op("dve", lambda e, xg=xg: e.tensor_tensor(
            xg, xg, S["pf0"][:], AluOpType.add), reads=["pf0", "X"], writes=["X"])


def build_stage5_program(W, consts, dump=()):
    ctx = Ctx("s5")
    ctx.consts = consts
    z_ap = ctx.inp("zdm", np.zeros((128, G * T), np.float32))
    x3_ap = ctx.inp("xdm3", np.zeros((128, G * T), np.float32))
    y_ap = ctx.out("y", (T, D))
    ctx.sb("X", (128, G * T), F32)
    ctx.pools(3, 4)
    ctx_common(ctx, consts)
    ctx.load(ctx.S["X"][:], z_ap[:, :], wr=["X"])
    emit_stage5(ctx, W, x3_ap)
    toks = emit_store_x_dmajor(ctx, y_ap, src="X")
    emit_dumps(ctx, dump)
    ctx.em.wait("pool", toks)
    ctx.em.build(ctx.stack)
    return ctx




def _run(ctx, overrides_list):
    in_maps = []
    for ov in overrides_list:
        m = {}
        for nm, (ap, arr) in ctx.ins.items():
            m[nm] = ov.get(nm, arr)
        in_maps.append(m)
    res = run_bass_kernel_spmd(ctx.nc, in_maps, core_ids=list(range(len(overrides_list))))
    return res.results


def kernel(**inputs):
    NB = 8
    x = np.ascontiguousarray(np.asarray(inputs["L0_attn_x"], np.float32))
    consts = make_constants()
    W1s = prep_weights(inputs)
    W2s = prep_weights2(inputs)
    W3s = prep_weights3(inputs)
    W4s = prep_weights4(inputs)
    W5s = prep_weights5(inputs)

    c1 = build_stage1_program((T, D), W1s, consts)
    r1 = _run(c1, [{"x": x[b]} for b in range(NB)])
    x1 = [np.asarray(r["xdm"], np.float32) for r in r1]

    c2 = build_stage2_program(W2s, consts)
    r2 = _run(c2, [{"xdm": x1[b]} for b in range(NB)])
    x2 = [np.asarray(r["xdm_o"], np.float32) for r in r2]

    c3 = build_stage3_program(W3s, consts)
    r3 = _run(c3, [{"xdm": x2[b]} for b in range(NB)])
    x3 = [np.asarray(r["xdm_o"], np.float32) for r in r3]

    c4 = build_stage4_program(W4s, consts)
    r4 = _run(c4, [{"xdm": x3[b]} for b in range(NB)])
    z1 = [np.asarray(r["zdm"], np.float32) for r in r4]

    c5 = build_stage5_program(W5s, consts)
    r5 = _run(c5, [{"zdm": z1[b], "xdm3": x3[b]} for b in range(NB)])
    y = np.stack([np.asarray(r["y"], np.float32) for r in r5])
    return y
